# version 12 (final)
# baseline (speedup 1.0000x reference)
import numpy as np
import ml_dtypes
import concourse.bass as bass
import concourse.mybir as mybir
from concourse.bass_utils import run_bass_kernel_spmd
from concourse.library_overlay import lower_extended_insts
from contextlib import ExitStack

F32 = mybir.dt.float32
BF16 = mybir.dt.bfloat16
AF = mybir.ActivationFunctionType
AX = mybir.AxisListType
OP = mybir.AluOpType

B, S, D = 4, 8192, 1024
NQ = 4096
NHP = 8
QS = 512
NQS = NQ // QS
NKB = S // 128
NGF = S // 512
NGQ = NQ // 512
EPS = 1e-5
SCALE = float(64) ** -0.5
LN101 = 0.009950331
ENGS = ("tensor", "vector", "scalar", "gpsimd", "sync")
TEN, VEC, SCA, POO, SYN = ENGS
MM, ADD, MAXO = OP.mult, OP.add, OP.max


class Prog:
    def __init__(self, nc, es):
        self.nc, self.es = nc, es
        self.ops = {e: [] for e in ENGS}
        self.sems, self.cnt = {}, {}

    def sem(self, name):
        self.sems[name] = self.es.enter_context(self.nc.semaphore(name))
        self.cnt[name] = 0

    def emit(self, eng, fn, inc=None, n=1):
        if inc is None:
            self.ops[eng].append(lambda e, fn=fn: fn(e))
            return None
        self.cnt[inc] += n
        h = self.sems[inc]
        self.ops[eng].append(lambda e, fn=fn, h=h, n=n: fn(e).then_inc(h, n))
        return self.cnt[inc]

    def wait(self, eng, sem, val):
        if val is None or val <= 0:
            return
        h = self.sems[sem]
        self.ops[eng].append(lambda e, h=h, v=val: e.wait_ge(h, v))

    def v(self, sem):
        return self.cnt[sem]


def build_nc():
    nc = bass.Bass("TRN2", target_bir_lowering=False)
    x_full = nc.dram_tensor("x_full", [S, D], F32, kind="ExternalInput").ap()
    x_q = nc.dram_tensor("x_q", [NQ, D], F32, kind="ExternalInput").ap()
    WIN = {w: nc.dram_tensor(w, [D, D], F32, kind="ExternalInput").ap()
           for w in ("wq", "wk", "wv", "wo", "wf")}
    bfi = nc.dram_tensor("bfi", [8, 128], F32, kind="ExternalInput").ap()
    ldi = nc.dram_tensor("ldi", [4, 100], F32, kind="ExternalInput").ap()
    lui = nc.dram_tensor("lui", [50, 4], F32, kind="ExternalInput").ap()
    dai = nc.dram_tensor("dai", [100, D], F32, kind="ExternalInput").ap()
    uai = nc.dram_tensor("uai", [D, 50], F32, kind="ExternalInput").ap()
    cst = nc.dram_tensor("cst", [128, 16], F32, kind="ExternalInput").ap()
    eyei = nc.dram_tensor("eyei", [128, 128], F32, kind="ExternalInput").ap()
    onesr = nc.dram_tensor("onesr", [1, 128], F32, kind="ExternalInput").ap()
    browi = nc.dram_tensor("browi", [1, D], F32, kind="ExternalInput").ap()
    out_q = nc.dram_tensor("out_q", [NQ, D], F32, kind="ExternalOutput").ap()
    kt_sp = nc.dram_tensor("kt_sp", [NHP, 128, S], BF16, kind="Internal").ap()
    v_sp = nc.dram_tensor("v_sp", [NHP, 128, S], BF16, kind="Internal").ap()
    ctx_sp = nc.dram_tensor("ctx_sp", [NHP, 128, NQ], BF16, kind="Internal").ap()
    wot_sp = nc.dram_tensor("wot_sp", [8, 128, D], BF16, kind="Internal").ap()
    wft_sp = nc.dram_tensor("wft_sp", [8, 128, D], BF16, kind="Internal").ap()

    es = ExitStack()
    with es:
        sb = lambda n, s_, d: es.enter_context(nc.sbuf_tensor(n, s_, d))
        psal = lambda n, s_: es.enter_context(nc.psum_tensor(n, s_, F32))
        QT = sb("QT", [128, 32768], BF16)
        AB = sb("AB", [128, 40960], BF16)
        AFA = sb("AFA", [128, 12544], F32)
        CSTF = sb("CSTF", [128, 16], F32)
        EYEF = sb("EYEF", [128, 128], F32)
        ONESRF = sb("ONESRF", [1, 128], F32)
        EYEB = sb("EYEB", [128, 128], BF16)
        ONESB = sb("ONESB", [128, 2], BF16)
        BLKB = sb("BLKB", [128, 2], BF16)
        PTB = (sb("PT0", [128, 1024], BF16), sb("PT1", [128, 1024], BF16))
        BIASC = sb("BIASC", [128, 1], F32)
        BSB = sb("BSB", [128, 8], F32)
        SETL = sb("SETL", [128, 4], F32)

        for val, col in ((EPS, 0), (1.0, 1), (0.0, 5), (-0.5, 6), (SCALE, 7),
                         (LN101, 8), (0.5, 9), (-1.0, 10), (2.0, 11)):
            nc.const_aps.register(F32, val, CSTF[:, col:col + 1])

        WTQ, WTK, WTV = AB[:, 0:8192], AB[:, 8192:16384], AB[:, 16384:24576]
        XG = (AB[:, 24576:28672], AB[:, 28672:32768])
        XLNB = (AB[:, 32768:33792], AB[:, 33792:34816])
        KSTG = (AB[:, 34816:35328], AB[:, 35328:35840])
        VSTG = (AB[:, 35840:36864], AB[:, 36864:37888])
        SQBUF = AB[:, 37888:38400]
        WBSTG = (AB[:, 38400:39424], AB[:, 39424:40448])
        WOSTG = AB[:, 33792:34816]
        WFSTG = (AB[:, 32768:33280], AB[:, 33280:33792])
        WTW = QT[:, 0:8192]
        KTB = (AB[:, 0:8192], AB[:, 8192:16384])
        VBB = (AB[:, 16384:24576], AB[:, 24576:32768])
        CTXS = (AB[:, 34816:35328], AB[:, 35328:35840])
        WOT, WFT = QT[:, 0:8192], QT[:, 8192:16384]
        CTX3 = (AB[:, 0:4096], AB[:, 4096:8192])
        XLN3 = (AB[:, 8192:9216], AB[:, 9216:10240])
        XG3 = AB[:, 24576:28672]
        XF = (AFA[:, 0:1024], AFA[:, 1024:2048])
        WSTG = (AFA[:, 2048:3072], AFA[:, 3072:4096])
        BRAW = AFA[0:8, 2048:2176]
        LNS = AFA[:, 4096:4108]
        MVT = AFA[:, 4108:4110]
        LNV = AFA[:, 4110:4111]
        RSTD = AFA[:, 4111:4112]
        NMR = AFA[:, 4112:4113]
        TMP2 = AFA[0:2, 4113:4114]
        KSTAT = AFA[0:2, 4116:4124]
        QSTAT = AFA[0:2, 4124:4132]
        SMX = AFA[0:1, 4132:4140]
        T1F = AFA[0:4, 5160:6184]
        T2F = AFA[0:50, 6184:7208]
        UAT = AFA[0:50, 7208:8232]
        LDT = AFA[0:100, 8232:8236]
        LUT = AFA[0:4, 8236:8286]
        DAF = AFA[0:100, 8286:9310]
        LDF = AFA[0:4, 9310:9410]
        LUF = AFA[0:50, 9410:9414]
        UAF = AFA[:, 9414:9814]
        RLF = AFA[:, 0:512]
        RLSB = AFA[0:1, 512:1536]
        XF3 = tuple(AFA[:, i * 1024:(i + 1) * 1024] for i in range(4))
        X2T = (AFA[:, 4136:5160], AFA[:, 5160:6184],
               AFA[:, 6184:7208], AFA[:, 7208:8232])
        ATTF = (AFA[:, 8286:8798], AFA[:, 8798:9310])
        FFNF = (AFA[:, 9414:9926], AFA[:, 9926:10438])
        LNS3 = AFA[:, 10438:10450]
        MVT3 = AFA[:, 10450:10452]
        LNV3 = AFA[:, 10452:10453]
        RSTD3 = AFA[:, 10453:10454]
        NMR3 = AFA[:, 10454:10455]
        BIASF = AFA[:, 10455:11479]
        BROWF = AFA[0:1, 11479:12503]

        PSA = psal("PSA", [128, 1024])
        PSB = psal("PSB", [128, 1024])
        PSC = psal("PSC", [128, 512])
        PSL = psal("PSL", [128, 512])
        PSR = psal("PSR", [128, 512])
        PST = psal("PST", [128, 512])
        PSS = (PSA, PSB)
        PSBH = (PSB[:, 0:512].bitcast(BF16), PSB[:, 512:1024].bitcast(BF16))

        p = Prog(nc, es)
        for s_ in ("sSY", "sGP", "sMM", "sQK", "sPV", "sACT", "sDVE",
                   "sPOOL", "sKV"):
            p.sem(s_)

        ZEROF = CSTF[0:1, 5:6]

        def acc_settle(cell, sem="sMM"):
            return p.emit(TEN, lambda e, cell=cell: e.matmul(
                cell, ONESRF[0:1, 0:128], ZEROF,
                start=False, stop=True), sem)

        p.emit(SYN, lambda e: e.dma_start(out=CSTF[:], in_=cst), "sSY", 16)
        p.emit(SYN, lambda e: e.dma_start(out=EYEF[:], in_=eyei), "sSY", 16)
        p.emit(SYN, lambda e: e.dma_start(out=ONESRF[:], in_=onesr), "sSY", 16)
        p.emit(SYN, lambda e: e.dma_start(out=BRAW, in_=bfi), "sSY", 16)
        p.emit(SYN, lambda e: e.dma_start(out=BROWF, in_=browi), "sSY", 16)
        p.emit(SYN, lambda e: e.dma_start(out=LDF[:], in_=ldi), "sSY", 16)
        p.emit(SYN, lambda e: e.dma_start(out=LUF[:], in_=lui), "sSY", 16)
        p.emit(SYN, lambda e: e.dma_start(out=DAF[:], in_=dai), "sSY", 16)
        for ec in range(8):
            p.emit(SYN, lambda e, ec=ec: e.dma_start(
                out=UAF[:, ec * 50:(ec + 1) * 50],
                in_=uai[ec * 128:(ec + 1) * 128, :]), "sSY", 16)
        v_small = p.v("sSY")
        p.wait(VEC, "sSY", v_small)
        p.emit(VEC, lambda e: e.tensor_copy(EYEB[:], EYEF[:]), "sDVE")
        p.emit(VEC, lambda e: e.tensor_copy(ONESB[:], CSTF[:, 1:3]), "sDVE")
        v_consts = p.emit(VEC, lambda e: e.tensor_copy(BLKB[:], CSTF[:, 3:5]), "sDVE")
        p.emit(VEC, lambda e: e.memset(KSTAT[:], 0.0), "sDVE")
        v_memset = p.emit(VEC, lambda e: e.memset(QSTAT[:], 0.0), "sDVE")

        p.wait(TEN, "sSY", v_small)
        v_btr = p.emit(TEN, lambda e: e.transpose(PST[:, 0:8], BRAW, EYEF[0:8, 0:8]), "sMM")
        p.emit(TEN, lambda e: e.transpose(PST[0:100, 16:20], LDF[:], EYEF[0:4, 0:4]), "sMM")
        p.emit(TEN, lambda e: e.transpose(PST[0:4, 24:74], LUF[:], EYEF[0:50, 0:50]), "sMM")
        m = p.emit(TEN, lambda e: e.transpose(PST[0:4, 24:74], LUF[:], EYEF[0:50, 0:50]), "sMM")
        p.wait(VEC, "sMM", m)
        p.emit(VEC, lambda e: e.tensor_copy(BSB[:], PST[:, 0:8]), "sDVE")
        p.emit(VEC, lambda e: e.tensor_copy(LDT[:], PST[0:100, 16:20]), "sDVE")
        ua_ev = p.emit(VEC, lambda e: e.tensor_copy(LUT[:], PST[0:4, 24:74]), "sDVE")
        for rnd in range(2):
            p.wait(TEN, "sDVE", ua_ev)
            for j in range(4):
                ec = rnd * 4 + j
                p.emit(TEN, lambda e, ec=ec, j=j: e.transpose(
                    PST[0:50, j * 128:(j + 1) * 128],
                    UAF[:, ec * 50:(ec + 1) * 50], EYEF[:]), "sMM")
            ec = rnd * 4 + 3
            mlast = p.emit(TEN, lambda e, ec=ec: e.transpose(
                PST[0:50, 3 * 128:4 * 128],
                UAF[:, ec * 50:(ec + 1) * 50], EYEF[:]), "sMM")
            p.wait(VEC, "sMM", mlast)
            for j in range(4):
                ec = rnd * 4 + j
                ua_ev = p.emit(VEC, lambda e, ec=ec, j=j: e.tensor_copy(
                    UAT[:, ec * 128:(ec + 1) * 128],
                    PST[0:50, j * 128:(j + 1) * 128]), "sDVE")

        t_ev = ua_ev
        for half in range(2):
            p.wait(TEN, "sDVE", t_ev)
            p.emit(TEN, lambda e, h=half: e.matmul(
                PST[0:4, 0:512], LDT[:], DAF[:, h * 512:(h + 1) * 512],
                start=True, stop=True), "sMM")
            m = p.emit(TEN, lambda e, h=half: e.matmul(
                PST[0:4, 0:512], LDT[:], DAF[:, h * 512:(h + 1) * 512],
                start=True, stop=True), "sMM")
            p.wait(VEC, "sMM", m)
            t_ev = p.emit(VEC, lambda e, h=half: e.tensor_copy(
                T1F[:, h * 512:(h + 1) * 512], PST[0:4, 0:512]), "sDVE")
        for half in range(2):
            p.wait(TEN, "sDVE", t_ev)
            p.emit(TEN, lambda e, h=half: e.matmul(
                PST[0:50, 0:512], LUT[:], T1F[:, h * 512:(h + 1) * 512],
                start=True, stop=True), "sMM")
            m = p.emit(TEN, lambda e, h=half: e.matmul(
                PST[0:50, 0:512], LUT[:], T1F[:, h * 512:(h + 1) * 512],
                start=True, stop=True), "sMM")
            p.wait(VEC, "sMM", m)
            t_ev = p.emit(VEC, lambda e, h=half: e.tensor_copy(
                T2F[:, h * 512:(h + 1) * 512], PST[0:50, 0:512]), "sDVE")
        v_t2 = t_ev

        ld_dma = {}
        cast_v = {}
        trp_v = {}
        evac_v = {}
        wo_dma_last = 0
        ld_i = 0
        for wname in ("wq", "wk", "wv", "wf", "wo"):
            for ec in range(8):
                par = ld_i % 2
                if ld_i == 0:
                    p.wait(SYN, "sMM", v_btr)
                if ld_i >= 2:
                    p.wait(SYN, "sDVE", cast_v[ld_i - 2])
                ld_dma[ld_i] = p.emit(SYN, lambda e, w=wname, ec=ec, par=par: e.dma_start(
                    out=WSTG[par], in_=WIN[w][ec * 128:(ec + 1) * 128, :]), "sSY", 16)
                p.wait(VEC, "sSY", ld_dma[ld_i])
                if ld_i >= 2:
                    p.wait(VEC, "sMM", trp_v[ld_i - 2])
                cast_v[ld_i] = p.emit(VEC, lambda e, par=par: e.tensor_copy(
                    WBSTG[par], WSTG[par]), "sDVE")
                p.wait(TEN, "sDVE", cast_v[ld_i])
                if ld_i >= 2:
                    sname, sval = evac_v[ld_i - 2]
                    p.wait(TEN, sname, sval)
                for dc in range(8):
                    p.emit(TEN, lambda e, par=par, dc=dc: e.transpose(
                        PSBH[par][:, dc * 128:(dc + 1) * 128],
                        WBSTG[par][:, dc * 128:(dc + 1) * 128], EYEB[:]), "sMM")
                mlast = p.emit(TEN, lambda e, par=par: e.transpose(
                    PSBH[par][:, 7 * 128:8 * 128],
                    WBSTG[par][:, 7 * 128:8 * 128], EYEB[:]), "sMM")
                trp_v[ld_i] = mlast
                if wname != "wo":
                    dst = {"wq": WTQ, "wk": WTK, "wv": WTV, "wf": WTW}[wname]
                    dv = dst.rearrange("q (dc e) -> q dc e", dc=8)[:, :, ec * 128:(ec + 1) * 128]
                    sv = PSBH[par].rearrange("q (dc e) -> q dc e", dc=8)
                    p.wait(SCA, "sMM", mlast)
                    ev = p.emit(SCA, lambda e, dv=dv, sv=sv: e.copy(dv, sv), "sACT")
                    evac_v[ld_i] = ("sACT", ev)
                else:
                    p.wait(SCA, "sMM", mlast)
                    if ec >= 1:
                        p.wait(SCA, "sGP", wo_dma_last)
                    ev = p.emit(SCA, lambda e, par=par: e.copy(
                        WOSTG[:, 0:512], PSBH[par][:, 0:512]), "sACT")
                    ev = p.emit(SCA, lambda e, par=par: e.copy(
                        WOSTG[:, 512:1024], PSBH[par][:, 512:1024]), "sACT")
                    evac_v[ld_i] = ("sACT", ev)
                    p.wait(POO, "sACT", ev)
                    for dc in range(8):
                        wo_dma_last = p.emit(POO, lambda e, dc=dc, ec=ec: e.dma_start(
                            out=wot_sp[dc, :, ec * 128:(ec + 1) * 128],
                            in_=WOSTG[:, dc * 128:(dc + 1) * 128]), "sGP", 16)
                ld_i += 1
        v_wt_act = p.v("sACT")

        wf_dma = 0
        wf_ev = 0
        for dc in range(8):
            p.wait(TEN, "sACT", v_wt_act)
            p.wait(TEN, "sDVE", v_t2)
            if dc >= 1:
                p.wait(TEN, "sDVE", wf_ev)
            p.emit(TEN, lambda e, dc=dc: e.matmul(
                PSR[:], T2F[:, dc * 128:(dc + 1) * 128], UAT[:, 0:512],
                start=True, stop=True), "sMM")
            p.emit(TEN, lambda e, dc=dc: e.matmul(
                PSL[:], T2F[:, dc * 128:(dc + 1) * 128], UAT[:, 512:1024],
                start=True, stop=True), "sMM")
            m2a = acc_settle(PSR[:, 0:1])
            m2 = p.emit(TEN, lambda e, dc=dc: e.matmul(
                PSL[:], T2F[:, dc * 128:(dc + 1) * 128], UAT[:, 512:1024],
                start=True, stop=True), "sMM")
            p.wait(VEC, "sMM", m2)
            if dc >= 1:
                p.wait(VEC, "sGP", wf_dma)
            p.emit(VEC, lambda e, dc=dc: e.tensor_tensor(
                WFSTG[0], PSR[:], WTW[:, dc * 1024:dc * 1024 + 512], ADD), "sDVE")
            wf_ev = p.emit(VEC, lambda e, dc=dc: e.tensor_tensor(
                WFSTG[1], PSL[:], WTW[:, dc * 1024 + 512:(dc + 1) * 1024], ADD), "sDVE")
            p.wait(POO, "sDVE", wf_ev)
            p.emit(POO, lambda e, dc=dc: e.dma_start(
                out=wft_sp[dc, :, 0:512], in_=WFSTG[0]), "sGP", 16)
            wf_dma = p.emit(POO, lambda e, dc=dc: e.dma_start(
                out=wft_sp[dc, :, 512:1024], in_=WFSTG[1]), "sGP", 16)
        v_weff_gp = wf_dma

        xdma = {}
        xln_v = {}
        trp_done = {}
        trp_ev = {}
        kst_dma = {}
        vst_dma = {}
        bank_rd = {0: ("sDVE", 0), 1: ("sDVE", 0)}
        psc_rd = v_memset
        stat_last = v_memset
        psa_i = [0]

        p.wait(VEC, "sGP", v_weff_gp)

        def ln_tile(src_ap, ti):
            par = ti % 2
            if ti >= 2:
                p.wait(SYN, "sDVE", xln_v[ti - 2])
            xdma[ti] = p.emit(SYN, lambda e, a=src_ap, par=par: e.dma_start(
                out=XF[par], in_=a), "sSY", 16)
            p.wait(VEC, "sSY", xdma[ti])
            p.emit(VEC, lambda e, par=par: e.bn_stats(LNS[:, 0:6], XF[par][:, 0:512]), "sDVE")
            d2 = p.emit(VEC, lambda e, par=par: e.bn_stats(LNS[:, 6:12], XF[par][:, 512:1024]), "sDVE")
            p.wait(VEC, "sDVE", d2)
            d3 = p.emit(VEC, lambda e: e.bn_aggr(MVT[:], LNS[:]), "sDVE")
            p.wait(SCA, "sDVE", d3)
            a1 = p.emit(SCA, lambda e: e.activation(LNV[:], MVT[:, 1:2], AF.Ln,
                                                    bias=EPS, scale=1.0), "sACT")
            p.wait(SCA, "sACT", a1)
            a2 = p.emit(SCA, lambda e: e.activation(RSTD[:], LNV[:], AF.Exp,
                                                    bias=0.0, scale=-0.5), "sACT")
            p.wait(VEC, "sACT", a2)
            d4 = p.emit(VEC, lambda e: e.scalar_tensor_tensor(
                out=NMR[:], in0=MVT[:, 0:1], scalar=-1.0, in1=RSTD[:],
                op0=MM, op1=MM), "sDVE")
            p.wait(VEC, "sDVE", d4)
            if ti >= 2:
                p.wait(VEC, "sMM", trp_done[ti - 2])
            p.emit(VEC, lambda e, par=par: e.tensor_scalar(
                out=XLNB[par], in0=XF[par], scalar1=RSTD[:], scalar2=NMR[:],
                op0=MM, op1=ADD), "sDVE")
            xln_v[ti] = p.emit(VEC, lambda e, par=par: e.tensor_copy(
                SETL[:, 0:1], XLNB[par][:, 1023:1024]), "sDVE")
            return par

        def transpose_tile(gpar, t, ti, par):
            p.wait(TEN, "sDVE", xln_v[ti])
            if ti >= 2:
                p.wait(TEN, "sACT", trp_ev[ti - 2])
            for dc in range(8):
                p.emit(TEN, lambda e, par=par, dc=dc: e.transpose(
                    PSBH[par][:, dc * 128:(dc + 1) * 128],
                    XLNB[par][:, dc * 128:(dc + 1) * 128], EYEB[:]), "sMM")
            mlast = p.emit(TEN, lambda e, par=par: e.transpose(
                PSBH[par][:, 7 * 128:8 * 128],
                XLNB[par][:, 7 * 128:8 * 128], EYEB[:]), "sMM")
            trp_done[ti] = mlast
            p.wait(SCA, "sMM", mlast)
            dv = XG[gpar].rearrange("q (dc i) -> q dc i", dc=8)[:, :, t * 128:(t + 1) * 128]
            sv = PSBH[par].rearrange("q (dc i) -> q dc i", dc=8)
            trp_ev[ti] = p.emit(SCA, lambda e, dv=dv, sv=sv: e.copy(dv, sv), "sACT")

        def stat_tail(kind, E, src_bf, ev):
            nonlocal psc_rd, stat_last
            p.wait(VEC, "sDVE", ev)
            p.emit(VEC, lambda e, s=src_bf: e.tensor_tensor(SQBUF[:], s, s, MM), "sDVE")
            d1 = p.emit(VEC, lambda e: e.tensor_copy(
                SETL[:, 0:1], SQBUF[:, 511:512]), "sDVE")
            p.wait(TEN, "sDVE", d1)
            p.wait(TEN, "sDVE", psc_rd)
            m = p.emit(TEN, lambda e: e.matmul(PSC[0:2, :], BLKB[:], SQBUF[:],
                                               start=True, stop=True), "sMM")
            p.wait(VEC, "sMM", m)
            d2 = p.emit(VEC, lambda e: e.tensor_reduce(TMP2[:], PSC[0:2, :],
                                                       axis=AX.X, op=MAXO), "sDVE")
            psc_rd = d2
            p.wait(VEC, "sDVE", d2)
            stt = KSTAT if kind == "k" else QSTAT
            stat_last = p.emit(VEC, lambda e, stt=stt, E=E: e.tensor_tensor(
                stt[:, E:E + 1], stt[:, E:E + 1], TMP2[:], MAXO), "sDVE")

        def proj_kq(kind, gi, gq, gpar, grp_ready):
            wt = WTK if kind == "k" else WTQ
            for E in range(8):
                bsel = psa_i[0] % 2
                psa_i[0] += 1
                out_ps = PSA[:, bsel * 512:(bsel + 1) * 512]
                sname, sval = bank_rd[bsel]
                p.wait(TEN, sname, sval)
                p.wait(TEN, "sACT", grp_ready)
                for dc in range(8):
                    p.emit(TEN, lambda e, wt=wt, dc=dc, E=E, out_ps=out_ps, gpar=gpar: e.matmul(
                        out_ps, wt[:, dc * 1024 + E * 128:dc * 1024 + (E + 1) * 128],
                        XG[gpar][:, dc * 512:(dc + 1) * 512],
                        start=(dc == 0), stop=(dc == 7)),
                        "sMM" if dc == 7 else None)
                acc_settle(out_ps[:, 0:1])
                mlast = p.v("sMM")
                p.wait(VEC, "sMM", mlast)
                if kind == "k":
                    ksi = gi * 8 + E
                    par = ksi % 2
                    if ksi >= 2:
                        p.wait(VEC, "sSY", kst_dma[ksi - 2])
                    p.emit(VEC, lambda e, par=par, out_ps=out_ps: e.tensor_copy(
                        KSTG[par], out_ps), "sDVE")
                    ev = p.emit(VEC, lambda e, par=par: e.tensor_copy(
                        SETL[:, 0:1], KSTG[par][:, 511:512]), "sDVE")
                    bank_rd[bsel] = ("sDVE", ev)
                    p.wait(SYN, "sDVE", ev)
                    kst_dma[ksi] = p.emit(SYN, lambda e, E=E, gi=gi, par=par: e.dma_start(
                        out=kt_sp[E, :, gi * 512:(gi + 1) * 512], in_=KSTG[par]), "sSY", 16)
                    stat_tail("k", E, KSTG[par], ev)
                else:
                    qv = QT[:, E * 4096 + gq * 512:E * 4096 + (gq + 1) * 512]
                    ev = p.emit(VEC, lambda e, qv=qv, out_ps=out_ps: e.tensor_scalar(
                        out=qv, in0=out_ps, scalar1=SCALE, scalar2=None,
                        op0=MM), "sDVE")
                    bank_rd[bsel] = ("sDVE", ev)
                    stat_tail("q", E, qv, ev)

        def proj_v(gi, gpar, grp_ready):
            for t in range(4):
                kb = gi * 4 + t
                par = kb % 2
                eva = None
                for h2 in range(2):
                    bsel = psa_i[0] % 2
                    psa_i[0] += 1
                    out_ps = PSA[:, bsel * 512:(bsel + 1) * 512]
                    sname, sval = bank_rd[bsel]
                    p.wait(TEN, sname, sval)
                    p.wait(TEN, "sACT", grp_ready)
                    for dc in range(8):
                        p.emit(TEN, lambda e, dc=dc, t=t, h2=h2, out_ps=out_ps, gpar=gpar: e.matmul(
                            out_ps,
                            XG[gpar][:, dc * 512 + t * 128:dc * 512 + (t + 1) * 128],
                            WTV[:, dc * 1024 + h2 * 512:dc * 1024 + (h2 + 1) * 512],
                            start=(dc == 0), stop=(dc == 7)),
                            "sMM" if dc == 7 else None)
                    acc_settle(out_ps[:, 0:1])
                    mlast = p.v("sMM")
                    p.wait(SCA, "sMM", mlast)
                    if kb >= 2 and h2 == 0:
                        p.wait(SCA, "sGP", vst_dma[kb - 2])
                    p.emit(SCA, lambda e, par=par, h2=h2, out_ps=out_ps: e.copy(
                        VSTG[par][:, h2 * 512:(h2 + 1) * 512], out_ps), "sACT")
                    eva = p.emit(SCA, lambda e, par=par, h2=h2: e.copy(
                        SETL[:, 1:2], VSTG[par][:, h2 * 512 + 511:h2 * 512 + 512]),
                        "sACT")
                    bank_rd[bsel] = ("sACT", eva)
                p.wait(POO, "sACT", eva)
                for hp in range(8):
                    vst_dma[kb] = p.emit(POO, lambda e, hp=hp, kb=kb, par=par: e.dma_start(
                        out=v_sp[hp, :, kb * 128:(kb + 1) * 128],
                        in_=VSTG[par][:, hp * 128:(hp + 1) * 128]), "sGP", 16)

        ti = 0
        for gi in range(NGF):
            gpar = gi % 2
            for t in range(4):
                par = ln_tile(x_full[ti * 128:(ti + 1) * 128, :], ti)
                transpose_tile(gpar, t, ti, par)
                ti += 1
            grp_ready = p.emit(SCA, lambda e, gpar=gpar: e.copy(
                SETL[:, 1:2], XG[gpar][:, 4095:4096]), "sACT")
            proj_kq("k", gi, None, gpar, grp_ready)
            proj_v(gi, gpar, grp_ready)
        for gq in range(NGQ):
            gi = NGF + gq
            gpar = gi % 2
            for t in range(4):
                par = ln_tile(x_q[gq * 512 + t * 128:gq * 512 + (t + 1) * 128, :], ti)
                transpose_tile(gpar, t, ti, par)
                ti += 1
            grp_ready = p.emit(SCA, lambda e, gpar=gpar: e.copy(
                SETL[:, 1:2], XG[gpar][:, 4095:4096]), "sACT")
            proj_kq("q", gi, gq, gpar, grp_ready)

        p.wait(POO, "sDVE", stat_last)
        p.emit(POO, lambda e: e.tensor_reduce(SMX[:, 0:1], KSTAT[:],
                                              axis=AX.XYZWC, op=MAXO), "sPOOL")
        g2 = p.emit(POO, lambda e: e.tensor_reduce(SMX[:, 1:2], QSTAT[:],
                                                   axis=AX.XYZWC, op=MAXO), "sPOOL")
        p.wait(SCA, "sPOOL", g2)
        a1 = p.emit(SCA, lambda e: e.activation(SMX[:, 2:3], SMX[:, 0:1], AF.Ln,
                                                bias=0.0, scale=1.0), "sACT")
        p.wait(SCA, "sACT", a1)
        a2 = p.emit(SCA, lambda e: e.activation(SMX[:, 3:4], SMX[:, 1:2], AF.Ln,
                                                bias=0.0, scale=1.0), "sACT")
        p.wait(VEC, "sACT", a2)
        d1 = p.emit(VEC, lambda e: e.tensor_tensor(SMX[:, 4:5], SMX[:, 2:3],
                                                   SMX[:, 3:4], ADD), "sDVE")
        p.wait(SCA, "sDVE", d1)
        a3 = p.emit(SCA, lambda e: e.activation(SMX[:, 5:6], SMX[:, 4:5], AF.Exp,
                                                bias=LN101, scale=0.5), "sACT")
        p.wait(VEC, "sACT", a3)
        d2 = p.emit(VEC, lambda e: e.tensor_scalar(
            out=SMX[:, 6:7], in0=SMX[:, 5:6], scalar1=-1.0, scalar2=None,
            op0=MM), "sDVE")
        p.wait(TEN, "sDVE", d2)
        p.wait(TEN, "sDVE", p.v("sDVE"))
        p.emit(TEN, lambda e: e.matmul(PST[:, 0:1], ONESRF[:], SMX[0:1, 6:7],
                                       start=True, stop=True), "sMM")
        mb = p.emit(TEN, lambda e: e.matmul(PST[:, 0:1], ONESRF[:], SMX[0:1, 6:7],
                                            start=True, stop=True), "sMM")
        p.wait(VEC, "sMM", mb)
        v_biasc = p.emit(VEC, lambda e: e.tensor_copy(BIASC[:], PST[:, 0:1]), "sDVE")
        p.wait(TEN, "sDVE", v_biasc)
        for bh in range(2):
            if bh == 1:
                p.wait(TEN, "sDVE", p.v("sDVE"))
            p.emit(TEN, lambda e, bh=bh: e.matmul(
                PST[:], ONESRF[0:1, 0:128], BROWF[0:1, bh * 512:(bh + 1) * 512],
                start=True, stop=True), "sMM")
            mbf = p.emit(TEN, lambda e, bh=bh: e.matmul(
                PST[:], ONESRF[0:1, 0:128], BROWF[0:1, bh * 512:(bh + 1) * 512],
                start=True, stop=True), "sMM")
            p.wait(VEC, "sMM", mbf)
            v_biasf = p.emit(VEC, lambda e, bh=bh: e.tensor_copy(
                BIASF[:, bh * 512:(bh + 1) * 512], PST[:]), "sDVE")

        v_p0_act = p.v("sACT")
        v_p0_mm = p.v("sMM")
        v_p0_sy = p.v("sSY")

        kv_load = {}
        p.wait(SYN, "sMM", v_p0_mm)
        for hp in range(NHP):
            if hp >= 2:
                p.wait(SYN, "sKV", hp - 1)
            p.emit(SYN, lambda e, hp=hp: e.dma_start(
                out=KTB[hp % 2], in_=kt_sp[hp]), "sSY", 16)
            kv_load[hp] = p.emit(SYN, lambda e, hp=hp: e.dma_start(
                out=VBB[hp % 2], in_=v_sp[hp]), "sSY", 16)

        NID = NHP * NQS * NKB
        exp_val = [0] * NID
        pv_val = [0] * NID
        qk_val = [0] * NID
        fin_val = {}
        ctx_st = {}
        last_rl_mm = 0
        last_rlf_ev = 0
        act_first = [True]

        def emit_exp(idx):
            p.wait(SCA, "sQK", qk_val[idx])
            if idx >= 2:
                p.wait(SCA, "sPV", pv_val[idx - 2])
            if act_first[0]:
                p.wait(SCA, "sDVE", v_biasf)
                act_first[0] = False
            exp_val[idx] = p.emit(SCA, lambda e, idx=idx: e.activation(
                PTB[idx % 2][:], PSS[idx % 2][:], AF.Exp,
                bias=BIASC[:], scale=1.0), "sACT")

        def emit_pv(idx):
            hp, r = divmod(idx, NQS * NKB)
            qsub, kb = divmod(r, NKB)
            p.wait(TEN, "sACT", exp_val[idx])
            if kb == 0:
                hq = hp * NQS + qsub
                if hq >= 1:
                    p.wait(TEN, "sDVE", fin_val[hq - 1])
            st = (kb == 0)
            sp = (kb == NKB - 1)
            pt = PTB[idx % 2]
            p.emit(TEN, lambda e, hp=hp, kb=kb, pt=pt, st=st, sp=sp: e.matmul(
                PSC[0:64, :], VBB[hp % 2][:, kb * 128:kb * 128 + 64],
                pt[:, 0:512], start=st, stop=sp, tile_position=(0, 0)), None)
            p.emit(TEN, lambda e, hp=hp, kb=kb, pt=pt, st=st, sp=sp: e.matmul(
                PSC[64:128, :], VBB[hp % 2][:, kb * 128 + 64:(kb + 1) * 128],
                pt[:, 512:1024], start=st, stop=sp, tile_position=(0, 64)), None)
            p.emit(TEN, lambda e, pt=pt, st=st, sp=sp: e.matmul(
                PSL[0:1, :], ONESB[:, 0:1], pt[:, 0:512],
                start=st, stop=sp, tile_position=(0, 0)), None)
            pv_val[idx] = p.emit(TEN, lambda e, pt=pt, st=st, sp=sp: e.matmul(
                PST[0:1, :], ONESB[:, 1:2], pt[:, 512:1024],
                start=st, stop=sp, tile_position=(0, 0)), "sPV")

        first_qk = [True]
        for hp in range(NHP):
            for qsub in range(NQS):
                hq = hp * NQS + qsub
                base = hq * NKB
                if qsub == 0:
                    p.wait(TEN, "sSY", kv_load[hp])
                qcol = hp * 4096 + qsub * 512
                for kb in range(NKB):
                    idx = base + kb
                    if idx >= 2:
                        p.wait(TEN, "sACT", exp_val[idx - 2])
                    elif first_qk[0]:
                        p.wait(TEN, "sACT", v_p0_act)
                        p.wait(TEN, "sDVE", v_biasf)
                        first_qk[0] = False
                    p.emit(TEN, lambda e, hp=hp, kb=kb, qcol=qcol, idx=idx: e.matmul(
                        PSS[idx % 2][:, 0:512],
                        KTB[hp % 2][0:64, kb * 128:(kb + 1) * 128],
                        QT[0:64, qcol:qcol + 512],
                        start=True, stop=True, tile_position=(0, 0)), None)
                    qk_val[idx] = p.emit(TEN, lambda e, hp=hp, kb=kb, qcol=qcol, idx=idx: e.matmul(
                        PSS[idx % 2][:, 512:1024],
                        KTB[hp % 2][64:128, kb * 128:(kb + 1) * 128],
                        QT[64:128, qcol:qcol + 512],
                        start=True, stop=True, tile_position=(64, 0)), "sQK")
                    emit_exp(idx)
                    if kb >= 1:
                        emit_pv(idx - 1)
                emit_pv(base + NKB - 1)
                p.wait(TEN, "sDVE", last_rlf_ev)
                acc_settle(PSC[:, 0:1], "sPV")
                acc_settle(PSL[:, 0:1], "sPV")
                pvs = acc_settle(PST[:, 0:1], "sPV")
                if qsub == NQS - 1:
                    p.emit(TEN, lambda e: e.nop(), "sKV")
                p.wait(VEC, "sPV", pvs)
                if hq == 0:
                    p.wait(VEC, "sSY", v_p0_sy)
                p.wait(VEC, "sMM", last_rl_mm)
                p.emit(VEC, lambda e: e.reciprocal_approx_fast(
                    out=RLSB[0:1, 0:512], in_=PSL[0:1, :]), "sDVE")
                recip = p.emit(VEC, lambda e: e.reciprocal_approx_fast(
                    out=RLSB[0:1, 512:1024], in_=PST[0:1, :]), "sDVE")
                p.wait(TEN, "sDVE", recip)
                p.emit(TEN, lambda e: e.matmul(
                    PSR[0:64, :], ONESRF[0:1, 0:64], RLSB[0:1, 0:512],
                    start=True, stop=True, tile_position=(0, 0)), None)
                p.emit(TEN, lambda e: e.matmul(
                    PSR[64:128, :], ONESRF[0:1, 0:64], RLSB[0:1, 512:1024],
                    start=True, stop=True, tile_position=(0, 64)), None)
                last_rl_mm = p.emit(TEN, lambda e: e.matmul(
                    PSR[64:128, :], ONESRF[0:1, 0:64], RLSB[0:1, 512:1024],
                    start=True, stop=True, tile_position=(0, 64)), "sMM")
                p.wait(VEC, "sMM", last_rl_mm)
                last_rlf_ev = p.emit(VEC, lambda e: e.tensor_copy(RLF[:], PSR[:]), "sDVE")
                p.wait(VEC, "sDVE", last_rlf_ev)
                if hq >= 2:
                    p.wait(VEC, "sGP", ctx_st[hq - 2])
                p.emit(VEC, lambda e, par=qsub % 2: e.tensor_tensor(
                    CTXS[par], PSC[:], RLF[:], MM), "sDVE")
                fin_val[hq] = p.emit(VEC, lambda e, par=qsub % 2: e.tensor_copy(
                    SETL[:, 0:1], CTXS[par][:, 511:512]), "sDVE")
                p.wait(POO, "sDVE", fin_val[hq])
                ctx_st[hq] = p.emit(POO, lambda e, hp=hp, qsub=qsub, par=qsub % 2: e.dma_start(
                    out=ctx_sp[hp, :, qsub * 512:(qsub + 1) * 512],
                    in_=CTXS[par]), "sGP", 16)

        v_last_hp3_qk = qk_val[(3 * NQS + NQS - 1) * NKB + NKB - 1]

        p.wait(SYN, "sQK", v_last_hp3_qk)
        for dc in range(8):
            p.emit(SYN, lambda e, dc=dc: e.dma_start(
                out=WOT.rearrange("q (a e) -> q a e", a=8)[:, dc, :],
                in_=wot_sp[dc]), "sSY", 16)
        for dc in range(8):
            wload = p.emit(SYN, lambda e, dc=dc: e.dma_start(
                out=WFT.rearrange("q (a e) -> q a e", a=8)[:, dc, :],
                in_=wft_sp[dc]), "sSY", 16)
        v_p3w = wload

        ctx3_ld = {}
        xq_ld = {}
        attn_add_last = {}
        ffn_mm_last = {}
        out_st = {}
        ln3_evac = {}
        ln3_tl = {}
        v_fin_all = fin_val[NHP * NQS - 1]
        bank3_rd = {0: ("sACT", 0), 1: ("sACT", 0)}
        xg3_rd = 0

        for qsub in range(NQS):
            par = qsub % 2
            p.wait(SYN, "sGP", ctx_st[7 * NQS + qsub])
            if qsub >= 2:
                p.wait(SYN, "sMM", ffn_mm_last[qsub - 2])
            for hp in range(8):
                ctx3_ld[qsub] = p.emit(SYN, lambda e, hp=hp, qsub=qsub, par=par: e.dma_start(
                    out=CTX3[par][:, hp * 512:(hp + 1) * 512],
                    in_=ctx_sp[hp, :, qsub * 512:(qsub + 1) * 512]), "sSY", 16)
            if qsub == 0:
                p.wait(SYN, "sDVE", v_fin_all)
            for t in range(4):
                if qsub >= 1:
                    p.wait(SYN, "sDVE", attn_add_last[qsub - 1])
                xq_ld[(qsub, t)] = p.emit(SYN, lambda e, qsub=qsub, t=t: e.dma_start(
                    out=XF3[t], in_=x_q[qsub * 512 + t * 128:qsub * 512 + (t + 1) * 128, :]),
                    "sSY", 16)
            if qsub == 0:
                p.wait(TEN, "sSY", v_p3w)
                p.wait(TEN, "sDVE", v_fin_all)
            for t in range(4):
                for h in range(2):
                    bsel = psa_i[0] % 2
                    psa_i[0] += 1
                    out_ps = PSA[:, bsel * 512:(bsel + 1) * 512]
                    sname, sval = bank3_rd[bsel]
                    p.wait(TEN, sname, sval)
                    if t == 0 and h == 0:
                        p.wait(TEN, "sSY", ctx3_ld[qsub])
                    for dc in range(8):
                        p.emit(TEN, lambda e, dc=dc, t=t, h=h, out_ps=out_ps, par=par: e.matmul(
                            out_ps,
                            CTX3[par][:, dc * 512 + t * 128:dc * 512 + (t + 1) * 128],
                            WOT[:, dc * 1024 + h * 512:dc * 1024 + (h + 1) * 512],
                            start=(dc == 0), stop=(dc == 7)),
                            "sMM" if dc == 7 else None)
                    acc_settle(out_ps[:, 0:1])
                    mlast = p.v("sMM")
                    p.wait(VEC, "sMM", mlast)
                    p.wait(VEC, "sSY", xq_ld[(qsub, t)])
                    if (qsub - 1, t) in out_st:
                        p.wait(VEC, "sGP", out_st[(qsub - 1, t)])
                    d = p.emit(VEC, lambda e, t=t, h=h, out_ps=out_ps: e.tensor_tensor(
                        X2T[t][:, h * 512:(h + 1) * 512], out_ps,
                        XF3[t][:, h * 512:(h + 1) * 512], ADD), "sDVE")
                    bank3_rd[bsel] = ("sDVE", d)
            attn_add_last[qsub] = d
            for t in range(4):
                tpar = t % 2
                p.wait(VEC, "sDVE", attn_add_last[qsub])
                p.emit(VEC, lambda e, t=t: e.bn_stats(LNS3[:, 0:6], X2T[t][:, 0:512]), "sDVE")
                d2 = p.emit(VEC, lambda e, t=t: e.bn_stats(LNS3[:, 6:12], X2T[t][:, 512:1024]), "sDVE")
                p.wait(VEC, "sDVE", d2)
                d3 = p.emit(VEC, lambda e: e.bn_aggr(MVT3[:], LNS3[:]), "sDVE")
                p.wait(SCA, "sDVE", d3)
                a1 = p.emit(SCA, lambda e: e.activation(LNV3[:], MVT3[:, 1:2], AF.Ln,
                                                        bias=EPS, scale=1.0), "sACT")
                p.wait(SCA, "sACT", a1)
                a2 = p.emit(SCA, lambda e: e.activation(RSTD3[:], LNV3[:], AF.Exp,
                                                        bias=0.0, scale=-0.5), "sACT")
                p.wait(VEC, "sACT", a2)
                d4 = p.emit(VEC, lambda e: e.scalar_tensor_tensor(
                    out=NMR3[:], in0=MVT3[:, 0:1], scalar=-1.0, in1=RSTD3[:],
                    op0=MM, op1=MM), "sDVE")
                p.wait(VEC, "sDVE", d4)
                p.wait(VEC, "sMM", ln3_tl.get((qsub, t - 2),
                                              ln3_tl.get((qsub - 1, t + 2), 0)))
                p.emit(VEC, lambda e, t=t, tpar=tpar: e.tensor_scalar(
                    out=XLN3[tpar], in0=X2T[t], scalar1=RSTD3[:], scalar2=NMR3[:],
                    op0=MM, op1=ADD), "sDVE")
                dx = p.emit(VEC, lambda e, tpar=tpar: e.tensor_copy(
                    SETL[:, 0:1], XLN3[tpar][:, 1023:1024]), "sDVE")
                p.wait(TEN, "sDVE", dx)
                if (qsub, t) != (0, 0):
                    p.wait(TEN, "sACT", ln3_evac.get((qsub, t - 2), ln3_evac.get((qsub - 1, t + 2), 0)))
                p.wait(TEN, "sMM", xg3_rd)
                for dc in range(8):
                    p.emit(TEN, lambda e, tpar=tpar, dc=dc: e.transpose(
                        PSBH[tpar][:, dc * 128:(dc + 1) * 128],
                        XLN3[tpar][:, dc * 128:(dc + 1) * 128], EYEB[:]),
                        "sMM" if dc == 7 else None)
                p.emit(TEN, lambda e, tpar=tpar: e.transpose(
                    PSBH[tpar][:, 7 * 128:8 * 128],
                    XLN3[tpar][:, 7 * 128:8 * 128], EYEB[:]), "sMM")
                tl = p.v("sMM")
                ln3_tl[(qsub, t)] = tl
                p.wait(SCA, "sMM", tl)
                dv = XG3.rearrange("q (dc i) -> q dc i", dc=8)[:, :, t * 128:(t + 1) * 128]
                sv = PSBH[tpar].rearrange("q (dc i) -> q dc i", dc=8)
                ln3_evac[(qsub, t)] = p.emit(SCA, lambda e, dv=dv, sv=sv: e.copy(dv, sv), "sACT")
            xg3_settle = p.emit(SCA, lambda e: e.copy(
                SETL[:, 1:2], XG3[:, 4095:4096]), "sACT")
            for t in range(4):
                for h in range(2):
                    bsel = psa_i[0] % 2
                    psa_i[0] += 1
                    out_ps = PSA[:, bsel * 512:(bsel + 1) * 512]
                    sname, sval = bank3_rd[bsel]
                    p.wait(TEN, sname, sval)
                    if t == 0 and h == 0:
                        p.wait(TEN, "sACT", xg3_settle)
                    for dc in range(8):
                        p.emit(TEN, lambda e, dc=dc, t=t, h=h, out_ps=out_ps: e.matmul(
                            out_ps,
                            XG3[:, dc * 512 + t * 128:dc * 512 + (t + 1) * 128],
                            WFT[:, dc * 1024 + h * 512:dc * 1024 + (h + 1) * 512],
                            start=(dc == 0), stop=(dc == 7)),
                            "sMM" if dc == 7 else None)
                    acc_settle(out_ps[:, 0:1])
                    mlast = p.v("sMM")
                    ffn_mm_last[qsub] = mlast
                    xg3_rd = mlast
                    p.wait(VEC, "sMM", mlast)
                    d1 = p.emit(VEC, lambda e, t=t, h=h, out_ps=out_ps: e.tensor_tensor(
                        X2T[t][:, h * 512:(h + 1) * 512], out_ps,
                        X2T[t][:, h * 512:(h + 1) * 512], ADD), "sDVE")
                    bank3_rd[bsel] = ("sDVE", d1)
                    p.emit(VEC, lambda e, t=t, h=h: e.tensor_tensor(
                        X2T[t][:, h * 512:(h + 1) * 512],
                        X2T[t][:, h * 512:(h + 1) * 512],
                        BIASF[:, h * 512:(h + 1) * 512], ADD), "sDVE")
                stt = p.emit(VEC, lambda e, t=t: e.tensor_copy(
                    SETL[:, 0:1], X2T[t][:, 1023:1024]), "sDVE")
                p.wait(POO, "sDVE", stt)
                out_st[(qsub, t)] = p.emit(POO, lambda e, qsub=qsub, t=t: e.dma_start(
                    out=out_q[qsub * 512 + t * 128:qsub * 512 + (t + 1) * 128, :],
                    in_=X2T[t]), "sGP", 16)

        p.wait(POO, "sGP", p.v("sGP"))
        p.wait(POO, "sSY", p.v("sSY"))
        p.wait(POO, "sACT", p.v("sACT"))
        p.wait(POO, "sDVE", p.v("sDVE"))
        p.wait(POO, "sMM", p.v("sMM"))
        p.wait(POO, "sQK", p.v("sQK"))
        p.wait(POO, "sPV", p.v("sPV"))
        p.wait(POO, "sKV", p.v("sKV"))
        p.wait(POO, "sPOOL", p.v("sPOOL"))

        with nc.Block() as block:
            @block.tensor
            def _(eng):
                for f in p.ops[TEN]:
                    f(eng)

            @block.vector
            def _(eng):
                for f in p.ops[VEC]:
                    f(eng)

            @block.scalar
            def _(eng):
                for f in p.ops[SCA]:
                    f(eng)

            @block.gpsimd
            def _(eng):
                for f in p.ops[POO]:
                    f(eng)

            @block.sync
            def _(eng):
                for f in p.ops[SYN]:
                    f(eng)
    return nc


_NC_CACHE = []


def kernel(L0_attn_hidden_states, L0_attn_Wq, L0_attn_Wk, L0_attn_Wv,
           L0_attn_Wo, L0_ffn_W, L0_ffn_b, L0_ffn_lora_down, L0_ffn_lora_up,
           L0_ffn_down_aux, L0_ffn_up_aux):
    x = np.ascontiguousarray(np.asarray(L0_attn_hidden_states, dtype=np.float32))
    f32 = lambda a: np.ascontiguousarray(np.asarray(a, dtype=np.float32))
    wq, wk, wv, wo = f32(L0_attn_Wq), f32(L0_attn_Wk), f32(L0_attn_Wv), f32(L0_attn_Wo)
    wf, bvec = f32(L0_ffn_W), f32(L0_ffn_b)
    ldw, luw = f32(L0_ffn_lora_down), f32(L0_ffn_lora_up)
    daw, uaw = f32(L0_ffn_down_aux), f32(L0_ffn_up_aux)

    cstv = np.zeros((128, 16), np.float32)
    for val, col in ((EPS, 0), (1.0, 1), (0.0, 5), (-0.5, 6), (SCALE, 7),
                     (LN101, 8), (0.5, 9), (-1.0, 10), (2.0, 11)):
        cstv[:, col] = val
    cstv[:, 2] = 1.0
    cstv[0:64, 3] = 1.0
    cstv[64:128, 4] = 1.0
    eyev = np.eye(128, dtype=np.float32)
    onesv = np.ones((1, 128), np.float32)
    braw = bvec.reshape(8, 128)

    if not _NC_CACHE:
        _NC_CACHE.append(build_nc())
        lower_extended_insts(_NC_CACHE[0])
    nc = _NC_CACHE[0]

    in_maps = []
    for c in range(8):
        b = c // 2
        half = c % 2
        in_maps.append({
            "x_full": x[b],
            "x_q": x[b, half * NQ:(half + 1) * NQ],
            "wq": wq, "wk": wk, "wv": wv, "wo": wo, "wf": wf,
            "bfi": braw, "browi": bvec.reshape(1, D),
            "ldi": ldw, "lui": luw, "dai": daw, "uai": uaw,
            "cst": cstv, "eyei": eyev, "onesr": onesv,
        })
    res = run_bass_kernel_spmd(nc, in_maps, core_ids=list(range(8)))
    out = np.empty((B, S, D), np.float32)
    for c in range(8):
        b = c // 2
        half = c % 2
        out[b, half * NQ:(half + 1) * NQ] = res.results[c]["out_q"]
    return out
